# version 04 (final)
# baseline (speedup 1.0000x reference)
import numpy as np
import ml_dtypes
import concourse.bass as bass
import concourse.mybir as mybir
from concourse.bass_utils import run_bass_kernel_spmd
from contextlib import ExitStack

BFNP = ml_dtypes.bfloat16
F32 = mybir.dt.float32
BF16 = mybir.dt.bfloat16
U32 = mybir.dt.uint32
I32 = mybir.dt.int32
AF = mybir.ActivationFunctionType
ALU = mybir.AluOpType

D = 4096
INTER = 8192
CONVD = 10240
PROJ = 18560
NH = 128
HP = 64
NS = 128
NG = 8
KC = 4
CH = 128
T = 2048
NCHUNK = T // CH
EPS = 1e-5
NEGBIG = -2000.0

DFC = D // 128
IFC = INTER // 128
TB = 512
NTB = T // TB

ENGINES = ("sync", "tensor", "vector", "scalar", "gpsimd")


class SemC:
    def __init__(self, h, name):
        self.h = h
        self.name = name
        self.n = 0

    def bump(self, k):
        self.n += k
        return self.n


class Plan:
    def __init__(self):
        self.ops = {e: [] for e in ENGINES}

    def add(self, eng, fn, waits=(), incs=(), tag=""):
        incs2 = []
        for s, k in incs:
            s.bump(k)
            incs2.append((s, k))
        self.ops[eng].append((tuple(waits), fn, tuple(incs2), tag))




def build_nc(cfg):
    nc = bass.Bass("TRN2", target_bir_lowering=False)
    st = {}

    def din(name, shape, dt=F32):
        st[name] = nc.dram_tensor(name, shape, dt, kind="ExternalInput").ap()
        return st[name]

    def dout(name, shape, dt=F32):
        st[name] = nc.dram_tensor(name, shape, dt, kind="ExternalOutput").ap()
        return st[name]

    extern = set(cfg.get("extern", ()))
    extin = set(cfg.get("extin", ()))

    def dint(name, shape, dt=F32, shared=False):
        kind = "Internal"
        if name in extern:
            kind = "ExternalOutput"
        elif name in extin:
            kind = "ExternalInput"
        st[name] = nc.dram_tensor(name, shape, dt, kind=kind,
                                  addr_space=("Shared" if shared else "Local")).ap()
        return st[name]

    din("xT", [D, T])
    din("winT", [D, PROJ])
    din("wfT", [D, D])
    din("woutT", [INTER, D])
    din("ffn_bias_fm", [128, DFC])
    din("conv_w", [128, (CONVD // 128) * KC])
    din("conv_b_fm", [128, CONVD // 128])
    din("dt_bias_col", [128, 1])
    din("A_log_col", [128, 1])
    din("D_rep_fm", [128, IFC])
    din("nw_fm", [128, IFC])
    din("kk_col", [128, 1])
    din("aa11", [1, 1])
    din("ones_col", [128, 1])
    din("ones_row", [1, 128])
    din("ident", [128, 128])
    din("amask", [128, 128], BF16)
    din("d_halo", [CONVD, 3])
    din("bmask", [128, 128], BF16)
    din("bmask4", [128, 512], BF16)
    din("d_S0", [NS, INTER])
    din("seedmask", [128, 6])
    din("halo_bf", [CONVD, 3], BF16)
    dout("outT", [D, T])
    dint("d_x1", [D, T])
    dint("d_gate", [INTER, T], BF16)
    dint("d_hs", [INTER, T], BF16)
    dint("d_BC", [2 * NG * NS, T], BF16)
    dint("d_ls", [NCHUNK, NS, INTER], BF16)
    dint("d_prep", [NCHUNK, 3, 128, 128])
    dint("d_acsf", [NCHUNK, 128, 128])
    dint("d_y", [T, INTER], BF16)
    dint("d_normed", [INTER, T], BF16)
    dint("d_x2", [D, T])
    dint("d_wbT", [D, D], BF16)
    dint("d_stot", [NS, INTER])
    dint("d_aseg", [128, 1])
    dint("d_dbc", [NCHUNK, 128, 128])
    dint("d_hbcpre", [CONVD, T], BF16)
    dint("d_dt", [128, T])
    for j in range(3):
        dint(f"d_ps{j}", [NS, INTER])
        dint(f"d_pa{j}", [128, 1])
    dint("d_woutTb", [INTER, D], BF16)

    dbg = cfg.get("debug_dump", ())
    plan = Plan()

    with ExitStack() as ctx:
        sems = {}

        def sem(name):
            s = SemC(ctx.enter_context(nc.semaphore(name)), name)
            sems[name] = s
            return s

        ARENA_BYTES = 212736
        arena = ctx.enter_context(
            nc.sbuf_tensor("arena", [128, ARENA_BYTES // 2], BF16))

        def carve(off, n, dt=F32, parts=(0, 128)):
            eb = 4 if dt in (F32, U32, I32) else 2
            assert off % 32 == 0 and (off + n * eb) <= ARENA_BYTES, (off, n)
            v = arena[:, off // 2:(off + n * eb) // 2]
            if dt != BF16:
                v = v.bitcast(dt)
            p0, p1 = parts
            if (p0, p1) != (0, 128):
                v = arena[p0:p1, off // 2:(off + n * eb) // 2]
                if dt != BF16:
                    v = v.bitcast(dt)
            return v

        C_BH = carve(0, DFC)
        C_KK = carve(256, 1)
        C_ONEC = carve(288, 1)
        C_AA = carve(320, 1, parts=(0, 1))
        C_ONER = carve(384, 128, parts=(0, 1))
        C_IDENT = carve(1024, 128)
        C_AM = carve(1536, 128, BF16)
        C_BM = carve(1792, 128, BF16)
        C_CW = carve(2048, (CONVD // 128) * KC)
        C_CB = carve(3392, CONVD // 128)
        C_DTB = carve(3712, 1)
        C_ALOG = carve(3744, 1)
        C_DREP = carve(3776, IFC)
        C_NW = carve(4032, IFC)
        C_ACOL = carve(4288, 1)
        C_EPS = carve(4320, 1)
        C_ONE1 = carve(4352, 1)
        C_BM4 = carve(4384, 512, BF16)
        C_SMK = carve(5408, 6)
        DT_SP = carve(5440, T)
        P0 = 13664
        RTOP = ARENA_BYTES - 131072
        assert P0 + 65536 <= RTOP

        ACT = carve(RTOP, DFC * T, BF16)
        st["_smk"] = C_SMK

        psb = [ctx.enter_context(nc.psum_tensor(f"psb{i}", [128, TB], F32))
               for i in range(8)]

        s_lx = sem("s_lx")
        s_lw = sem("s_lw")
        s_pe = sem("s_pe")
        s_ve = sem("s_ve")
        s_ac = sem("s_ac")
        s_st = sem("s_st")
        s_g = sem("s_g")
        s_tr = sem("s_tr")
        s_d1 = sem("s_d1")
        s_d2 = sem("s_d2")
        s_d3 = sem("s_d3")
        s_d4 = sem("s_d4")
        s_d5 = sem("s_d5")

        ALL_SEMS = list(sems.values())

        def barrier(tag):
            vals = [(s, s.n) for s in ALL_SEMS]
            for e in ENGINES:
                plan.add(e, lambda eng: None, waits=vals, tag=f"bar:{tag}")

        st["_serial"] = False
        _orig_add = plan.add

        def _ser_add(eng, fn, waits=(), incs=(), tag=""):
            w = list(waits)
            ser = st["_serial"]
            if ser and (ser is True or any(p in tag for p in ser)):
                w += [(s, s.n) for s in ALL_SEMS]
            return _orig_add(eng, fn, waits=w, incs=incs, tag=tag)
        plan.add = _ser_add

        def dma(eng, out_ap, in_ap, sem_, k=16, waits=(), transpose=False, tag=""):
            def fn(e, out_ap=out_ap, in_ap=in_ap, transpose=transpose):
                return e.dma_start(out=out_ap, in_=in_ap, transpose=transpose)
            plan.add(eng, fn, waits=waits, incs=[(sem_, k)], tag=tag)
            return sem_.n

        dma("sync", C_KK, st["kk_col"][:, :], s_g)
        dma("sync", C_AA, st["aa11"][:, :], s_g)
        dma("sync", C_ONEC, st["ones_col"][:, :], s_g)
        dma("sync", C_ONER, st["ones_row"][:, :], s_g)
        dma("sync", C_IDENT, st["ident"][:, :], s_g)
        dma("sync", C_AM, st["amask"][:, :], s_g)
        dma("sync", C_BM, st["bmask"][:, :], s_g)
        dma("sync", C_BM4, st["bmask4"][:, :], s_g)
        dma("sync", C_SMK, st["seedmask"][:, :], s_g)
        dma("sync", C_CW, st["conv_w"][:, :], s_g)
        dma("sync", C_CB, st["conv_b_fm"][:, :], s_g)
        dma("sync", C_DTB, st["dt_bias_col"][:, :], s_g)
        dma("sync", C_ALOG, st["A_log_col"][:, :], s_g)
        dma("sync", C_DREP, st["D_rep_fm"][:, :], s_g)
        dma("sync", C_NW, st["nw_fm"][:, :], s_g)
        dma("sync", C_BH, st["ffn_bias_fm"][:, :], s_g)
        ld0 = s_g.n
        plan.add("vector",
                 lambda e: e.tensor_scalar(C_BH, C_BH, 0.5, None, ALU.mult),
                 waits=[(s_g, ld0)], incs=[(s_ve, 1)], tag="bias_half")
        plan.add("vector", lambda e: e.memset(C_EPS, EPS), incs=[(s_ve, 1)])
        plan.add("vector", lambda e: e.memset(C_ONE1, 1.0), incs=[(s_ve, 1)])
        barrier("consts")
        plan.add("scalar",
                 lambda e: e.activation(C_ACOL, C_ALOG, AF.Exp, scale=1.0),
                 waits=[(s_g, ld0)], incs=[(s_ac, 1)], tag="expA")
        plan.add("vector",
                 lambda e: e.tensor_scalar(C_ACOL, C_ACOL, -1.0, None, ALU.mult),
                 waits=[(s_ac, s_ac.n)], incs=[(s_ve, 1)], tag="negA")

        XA = carve(P0, T)
        XB = carve(P0 + 8192, T)
        SQ = carve(P0 + 16384, T)
        T1 = carve(P0 + 24576, T)
        ABC = carve(P0 + 32768, T, BF16)
        BBC = carve(P0 + 36864, T, BF16)
        ROWM = carve(RTOP, T, parts=(0, 1))
        ROWQ = carve(RTOP + 8192, T, parts=(0, 1))
        ROWT = carve(RTOP + 16384, T, parts=(0, 1))
        ROWA = ROWQ
        ROWB = ROWT

        def emit_layernorm(src, use_aa, tag):
            for fc in range(DFC):
                xt = XA if fc % 2 == 0 else XB
                w = []
                if fc >= 2:
                    w = [(s_pe, st[f"_ln{tag}pe{fc-2}"])]
                dma("gpsimd", xt, src[fc * 128:(fc + 1) * 128, :], s_lx,
                    waits=w, tag=f"ln{tag}ldx{fc}")
                ldn = s_lx.n
                sq_w = [(s_lx, ldn)]
                if fc > 0:
                    sq_w.append((s_pe, st[f"_ln{tag}pe{fc-1}"]))
                plan.add("scalar",
                         lambda e, xt=xt: e.activation(SQ, xt, AF.Square),
                         waits=sq_w, incs=[(s_ac, 1)], tag=f"ln{tag}sq{fc}")
                acn = s_ac.n
                for tb in range(NTB):
                    plan.add("tensor",
                             lambda e, xt=xt, tb=tb, fc=fc: e.matmul(
                                 psb[tb][0:1, :], C_ONEC,
                                 xt[:, tb * TB:(tb + 1) * TB],
                                 start=(fc == 0), stop=(fc == DFC - 1)),
                             waits=[(s_lx, ldn)] if tb == 0 else (),
                             tag=f"ln{tag}s{fc}.{tb}")
                for tb in range(NTB):
                    plan.add("tensor",
                             lambda e, tb=tb, fc=fc: e.matmul(
                                 psb[4 + tb][0:1, :], C_ONEC,
                                 SQ[:, tb * TB:(tb + 1) * TB],
                                 start=(fc == 0), stop=(fc == DFC - 1)),
                             waits=[(s_ac, acn)] if tb == 0 else (),
                             incs=[(s_pe, 1)] if tb == NTB - 1 else (),
                             tag=f"ln{tag}q{fc}.{tb}")
                st[f"_ln{tag}pe{fc}"] = s_pe.n
            pe_done = s_pe.n

            def rowops(e):
                for tb in range(NTB):
                    sl = slice(tb * TB, (tb + 1) * TB)
                    e.tensor_scalar(ROWM[:, sl], psb[tb][0:1, :], 1.0 / D, None, ALU.mult)
                    e.tensor_scalar(ROWQ[:, sl], psb[4 + tb][0:1, :], 1.0 / D, None, ALU.mult)
                e.drain()
                e.tensor_tensor(ROWT, ROWM, ROWM, ALU.mult)
                e.drain()
                return e.tensor_tensor(ROWQ, ROWQ, ROWT, ALU.subtract)
            plan.add("vector", rowops, waits=[(s_pe, pe_done)], incs=[(s_ve, 1)],
                     tag=f"ln{tag}rows")
            ven = s_ve.n
            plan.add("scalar",
                     lambda e: e.activation(ROWT, ROWQ, AF.Ln, bias=C_EPS[0:1, :]),
                     waits=[(s_ve, ven)], incs=[(s_ac, 1)], tag=f"ln{tag}ln")
            plan.add("scalar",
                     lambda e: e.activation(ROWA, ROWT, AF.Exp, scale=-0.5),
                     waits=[(s_ac, s_ac.n)],
                     incs=[(s_ac, 1)], tag=f"ln{tag}exp")
            acn = s_ac.n

            def rowops2(e):
                if use_aa:
                    e.tensor_scalar(ROWA, ROWA, C_AA, None, ALU.mult)
                    e.drain()
                e.tensor_tensor(ROWB, ROWM, ROWA, ALU.mult)
                e.drain()
                return e.tensor_scalar(ROWB, ROWB, -1.0, None, ALU.mult)
            plan.add("vector", rowops2, waits=[(s_ac, acn)], incs=[(s_ve, 1)],
                     tag=f"ln{tag}rows2")
            ven = s_ve.n
            for tb in range(NTB):
                plan.add("tensor",
                         lambda e, tb=tb: e.matmul(
                             psb[tb][:, :], C_ONER, ROWA[:, tb * TB:(tb + 1) * TB],
                             start=True, stop=True),
                         waits=[(s_ve, ven)] if tb == 0 else (), tag=f"ln{tag}bca{tb}")
                plan.add("tensor",
                         lambda e, tb=tb: e.matmul(
                             psb[4 + tb][:, :], C_ONER, ROWB[:, tb * TB:(tb + 1) * TB],
                             start=True, stop=True),
                         incs=[(s_pe, 1)] if tb == NTB - 1 else (), tag=f"ln{tag}bcb{tb}")
            pen = s_pe.n

            def evb(e):
                inst = None
                for tb in range(NTB):
                    sl = slice(tb * TB, (tb + 1) * TB)
                    e.tensor_copy(ABC[:, sl], psb[tb][:, :])
                    inst = e.tensor_copy(BBC[:, sl], psb[4 + tb][:, :])
                return inst
            plan.add("vector", evb, waits=[(s_pe, pen)], incs=[(s_ve, 1)],
                     tag=f"ln{tag}evb")
            ven = s_ve.n
            for fc in range(DFC):
                xt = XA if fc % 2 == 0 else XB
                w = [(s_ve, ven)]
                if fc >= 2:
                    w.append((s_ve, st[f"_ln{tag}ap{fc-2}"]))
                dma("gpsimd", xt, src[fc * 128:(fc + 1) * 128, :], s_lx,
                    waits=w, tag=f"ln{tag}ldx2.{fc}")
                ldn = s_lx.n

                def apply(e, xt=xt, fc=fc):
                    e.tensor_tensor(T1, xt, ABC, ALU.mult)
                    e.drain()
                    return e.tensor_tensor(ACT[:, fc * T:(fc + 1) * T], T1, BBC,
                                           ALU.add)
                plan.add("vector", apply, waits=[(s_lx, ldn)], incs=[(s_ve, 1)],
                         tag=f"ln{tag}ap{fc}")
                st[f"_ln{tag}ap{fc}"] = s_ve.n

        WB0 = carve(P0 + 49152, D, BF16)
        WB1 = carve(P0 + 57344, D, BF16)

        def emit_ffn(src_res, dst, tag, cache_wb=None, use_cached=False):
            for oc in range(DFC):
                wb = WB0 if oc % 2 == 0 else WB1
                w = []
                if oc >= 2:
                    w.append((s_pe, st[f"_f{tag}pe{oc-2}"]))
                    if f"_f{tag}cw{oc-2}" in st:
                        w.append((s_st, st[f"_f{tag}cw{oc-2}"]))
                if use_cached:
                    dma("sync", wb.rearrange("p (fc o) -> p fc o", o=128),
                        st["d_wbT"][:, oc * 128:(oc + 1) * 128].rearrange(
                            "(fc p) o -> p fc o", p=128),
                        s_lw, waits=w, tag=f"f{tag}lw{oc}")
                    mm0_wait = [(s_lw, s_lw.n)]
                else:
                    dma("gpsimd", wb.rearrange("p (fc o) -> p fc o", o=128),
                        st["wfT"][:, oc * 128:(oc + 1) * 128].rearrange(
                            "(fc p) o -> p fc o", p=128),
                        s_lw, waits=w, tag=f"f{tag}lw{oc}")
                    ldn = s_lw.n
                    plan.add("scalar",
                             lambda e, wb=wb: e.activation(
                                 wb, wb, AF.Tanh, scale=C_KK),
                             waits=[(s_lw, ldn)],
                             incs=[(s_ac, 1)], tag=f"f{tag}tanh{oc}")
                    acn = s_ac.n
                    mm0_wait = [(s_ac, acn)]
                    if cache_wb is not None:
                        dma("gpsimd", cache_wb[:, oc * 128:(oc + 1) * 128].rearrange(
                            "(fc p) o -> p fc o", p=128),
                            wb.rearrange("p (fc o) -> p fc o", o=128), s_st,
                            waits=[(s_ac, acn)], tag=f"f{tag}cw{oc}")
                        st[f"_f{tag}cw{oc}"] = s_st.n
                if oc >= 2:
                    mm0_wait.append((s_ac, st[f"_f{tag}ac{oc-2}"]))
                grp = (oc % 2) * 4
                for fc in range(DFC):
                    for tb in range(NTB):
                        plan.add("tensor",
                                 lambda e, wb=wb, fc=fc, tb=tb, grp=grp: e.matmul(
                                     psb[grp + tb][:, :],
                                     wb[:, fc * 128:(fc + 1) * 128],
                                     ACT[:, fc * T + tb * TB: fc * T + (tb + 1) * TB],
                                     start=(fc == 0), stop=(fc == DFC - 1)),
                                 waits=(mm0_wait if (fc == 0 and tb == 0) else ()),
                                 incs=([(s_pe, 1)] if (fc == DFC - 1 and tb == NTB - 1)
                                       else ()),
                                 tag=f"f{tag}mm{oc}.{fc}.{tb}")
                st[f"_f{tag}pe{oc}"] = s_pe.n
                pen = s_pe.n
                xt = XA if oc % 2 == 0 else XB
                w2 = []
                if oc >= 2:
                    w2.append((s_ve, st[f"_f{tag}ve{oc-2}"]))
                dma("gpsimd", xt, src_res[oc * 128:(oc + 1) * 128, :], s_lx,
                    waits=w2, tag=f"f{tag}lr{oc}")
                ldn2 = s_lx.n
                for tb in range(NTB):
                    ev_w = []
                    if tb == 0:
                        ev_w = [(s_pe, pen)]
                        if oc >= 1:
                            ev_w.append((s_ve, st[f"_f{tag}ve{oc-1}"]))
                    plan.add("scalar",
                             lambda e, grp=grp, tb=tb, oc=oc: e.activation(
                                 SQ[:, tb * TB:(tb + 1) * TB], psb[grp + tb][:, :],
                                 AF.Identity, scale=0.5, bias=C_BH[:, oc:oc + 1]),
                             waits=ev_w,
                             incs=([(s_ac, 1)] if tb == NTB - 1 else ()),
                             tag=f"f{tag}ev{oc}.{tb}")
                acn2 = s_ac.n
                st[f"_f{tag}ac{oc}"] = acn2
                res_w = [(s_lx, ldn2), (s_ac, acn2)]
                if oc >= 1:
                    res_w.append((s_st, st[f"_f{tag}stm{oc-1}"]))

                def resid(e, xt=xt):
                    return e.tensor_tensor(T1, SQ, xt, ALU.add)
                plan.add("vector", resid, waits=res_w, incs=[(s_ve, 1)],
                         tag=f"f{tag}res{oc}")
                st[f"_f{tag}ve{oc}"] = s_ve.n
                dma("gpsimd", dst[oc * 128:(oc + 1) * 128, :], T1, s_st,
                    waits=[(s_ve, s_ve.n)], tag=f"f{tag}st{oc}")
                st[f"_f{tag}stm{oc}"] = s_st.n

        HPRE = carve(P0, 2080, BF16)
        CT1 = carve(P0 + 4160, T, BF16)
        CT2 = carve(P0 + 8256, T, BF16)
        OUTT = carve(P0 + 12352, T, BF16)
        GT = carve(P0 + 16448, T, BF16)

        def emit_inproj(halo_src, tag="i1", do_conv=True):
            ocs = [144] + list(range(64, 144)) + list(range(0, 64))
            for i, oc in enumerate(ocs):
                wb = WB0 if i % 2 == 0 else WB1
                w = []
                if i >= 2:
                    w.append((s_pe, st[f"_{tag}pe{i-2}"]))
                dma("gpsimd", wb.rearrange("p (fc o) -> p fc o", o=128),
                    st["winT"][:, oc * 128:(oc + 1) * 128].rearrange(
                        "(fc p) o -> p fc o", p=128),
                    s_lw, waits=w, tag=f"{tag}lw{i}")
                mm0_wait = [(s_lw, s_lw.n)]
                if i >= 2:
                    mm0_wait.append((s_ve, st[f"_{tag}ev{i-2}"]))
                grp = (i % 2) * 4
                for fc in range(DFC):
                    for tb in range(NTB):
                        plan.add("tensor",
                                 lambda e, wb=wb, fc=fc, tb=tb, grp=grp: e.matmul(
                                     psb[grp + tb][:, :],
                                     wb[:, fc * 128:(fc + 1) * 128],
                                     ACT[:, fc * T + tb * TB: fc * T + (tb + 1) * TB],
                                     start=(fc == 0), stop=(fc == DFC - 1)),
                                 waits=(mm0_wait if (fc == 0 and tb == 0) else ()),
                                 incs=([(s_pe, 1)] if (fc == DFC - 1 and tb == NTB - 1)
                                       else ()),
                                 tag=f"{tag}mm{i}.{fc}.{tb}")
                st[f"_{tag}pe{i}"] = s_pe.n
                pen = s_pe.n

                if oc == 144:
                    def evdt(e, grp=grp):
                        inst = None
                        for tb in range(NTB):
                            inst = e.tensor_copy(
                                DT_SP[:, tb * TB:(tb + 1) * TB], psb[grp + tb][:, :])
                        return inst
                    plan.add("vector", evdt, waits=[(s_pe, pen)], incs=[(s_ve, 1)],
                             tag=f"{tag}evdt")
                    st[f"_{tag}ev{i}"] = s_ve.n
                elif oc >= 64 and not do_conv:
                    j = oc - 64
                    hv_w = [(s_pe, pen)]
                    if f"_{tag}hst_prev" in st:
                        hv_w.append((s_st, st[f"_{tag}hst_prev"]))

                    def evhp(e, grp=grp):
                        inst = None
                        for tb in range(NTB):
                            inst = e.tensor_copy(
                                HPRE[:, 3 + tb * TB:3 + (tb + 1) * TB],
                                psb[grp + tb][:, :])
                        return inst
                    plan.add("vector", evhp, waits=hv_w, incs=[(s_ve, 1)],
                             tag=f"{tag}evhp{j}")
                    st[f"_{tag}ev{i}"] = s_ve.n
                    dma("gpsimd", st["d_hbcpre"][j * 128:(j + 1) * 128, :],
                        HPRE[:, 3:3 + T], s_st, waits=[(s_ve, s_ve.n)],
                        tag=f"{tag}hpst{j}")
                    st[f"_{tag}hst_prev"] = s_st.n
                elif oc >= 64:
                    j = oc - 64
                    hw = []
                    if f"_{tag}cvlast" in st:
                        hw.append((s_ve, st[f"_{tag}cvlast"]))
                    dma("gpsimd", HPRE[:, 0:3],
                        halo_src[j * 128:(j + 1) * 128, :], s_lx, waits=hw,
                        tag=f"{tag}halo{j}")
                    ldh = s_lx.n

                    def evh(e, grp=grp):
                        inst = None
                        for tb in range(NTB):
                            inst = e.tensor_copy(
                                HPRE[:, 3 + tb * TB:3 + (tb + 1) * TB],
                                psb[grp + tb][:, :])
                        return inst
                    plan.add("vector", evh, waits=[(s_pe, pen)], incs=[(s_ve, 1)],
                             tag=f"{tag}evh{j}")
                    st[f"_{tag}ev{i}"] = s_ve.n

                    def conv(e, j=j):
                        e.tensor_scalar(CT1, HPRE[:, 0:T], C_CW[:, 4 * j:4 * j + 1],
                                        None, ALU.mult)
                        e.drain()
                        e.scalar_tensor_tensor(CT2, HPRE[:, 1:1 + T],
                                               C_CW[:, 4 * j + 1:4 * j + 2], CT1,
                                               ALU.mult, ALU.add)
                        e.drain()
                        e.scalar_tensor_tensor(CT1, HPRE[:, 2:2 + T],
                                               C_CW[:, 4 * j + 2:4 * j + 3], CT2,
                                               ALU.mult, ALU.add)
                        e.drain()
                        return e.scalar_tensor_tensor(CT2, HPRE[:, 3:3 + T],
                                                      C_CW[:, 4 * j + 3:4 * j + 4], CT1,
                                                      ALU.mult, ALU.add)
                    cw_w = [(s_lx, ldh)]
                    if f"_{tag}si_prev" in st:
                        cw_w.append((s_ac, st[f"_{tag}si_prev"]))
                    plan.add("vector", conv, waits=cw_w, incs=[(s_ve, 1)],
                             tag=f"{tag}cv{j}")
                    st[f"_{tag}cvlast"] = s_ve.n
                    si_w = [(s_ve, s_ve.n)]
                    if f"_{tag}sst_prev" in st:
                        si_w.append((s_st, st[f"_{tag}sst_prev"]))
                    plan.add("scalar",
                             lambda e, j=j: e.activation(
                                 OUTT, CT2, AF.Silu, bias=C_CB[:, j:j + 1]),
                             waits=si_w, incs=[(s_ac, 1)], tag=f"{tag}si{j}")
                    st[f"_{tag}si_prev"] = s_ac.n
                    if j < 64:
                        dst = st["d_hs"][j * 128:(j + 1) * 128, :]
                    else:
                        dst = st["d_BC"][(j - 64) * 128:(j - 63) * 128, :]
                    dma("gpsimd", dst, OUTT, s_st, waits=[(s_ac, s_ac.n)],
                        tag=f"{tag}sto{j}")
                    st[f"_{tag}sst_prev"] = s_st.n
                else:
                    gv_w = [(s_pe, pen)]
                    if f"_{tag}gst_prev" in st:
                        gv_w.append((s_st, st[f"_{tag}gst_prev"]))

                    def evg(e, grp=grp):
                        inst = None
                        for tb in range(NTB):
                            inst = e.tensor_copy(
                                GT[:, tb * TB:(tb + 1) * TB], psb[grp + tb][:, :])
                        return inst
                    plan.add("vector", evg, waits=gv_w, incs=[(s_ve, 1)],
                             tag=f"{tag}evg{oc}")
                    st[f"_{tag}ev{i}"] = s_ve.n
                    dma("gpsimd", st["d_gate"][oc * 128:(oc + 1) * 128, :], GT,
                        s_st, waits=[(s_ve, s_ve.n)], tag=f"{tag}stg{oc}")
                    st[f"_{tag}gst_prev"] = s_st.n

        HS_TM = carve(P0, INTER, BF16)
        HS2 = carve(P0 + 16384, INTER, BF16)
        B_TM = carve(P0 + 32768, NG * NS, BF16)
        LSB = carve(P0 + 34816, INTER, BF16)
        PR_ACS = carve(P0 + 51200, 128)
        PR_DS = carve(P0 + 51712, 128)
        PR_LN = carve(P0 + 52224, 128)
        PR_SDO = carve(P0 + 52736, 128)
        PR_CO = carve(P0 + 53248, 128)
        PR_DD = carve(P0 + 53760, 128)
        TRT = carve(P0 + 54272, 128)
        TRD = carve(P0 + 54784, 128)
        DROW = carve(P0 + 55296, 128, parts=(0, 1))
        DBC = carve(P0 + 55808, 128)
        ASEG = carve(P0 + 56320, 1)
        DECC = carve(P0 + 56352, 1)
        ADTF = carve(P0 + 57344, T)
        S_RUN = carve(RTOP, INTER)
        S_BF = carve(RTOP + 32768, INTER, BF16)
        YC = carve(RTOP + 49152, INTER, BF16)
        LS_IN = carve(RTOP + 65536, INTER, BF16)

        def emit_ssda(tag="sa"):
            def dtp1(e):
                e.tensor_scalar(ADTF, DT_SP, C_DTB, None, ALU.add)
                e.drain()
                return e.tensor_scalar(HS2.bitcast(F32)[:, 0:T], ADTF, 0.0,
                                       None, ALU.max)
            plan.add("vector", dtp1, incs=[(s_ve, 1)], tag=f"{tag}dtp1")
            ven = s_ve.n
            plan.add("scalar",
                     lambda e: e.activation(HS_TM.bitcast(F32)[:, 0:T],
                                            ADTF, AF.Abs),
                     waits=[(s_ve, ven)], incs=[(s_ac, 1)], tag=f"{tag}dtp1b")
            plan.add("scalar",
                     lambda e: e.activation(HS_TM.bitcast(F32)[:, 0:T],
                                            HS_TM.bitcast(F32)[:, 0:T],
                                            AF.Exp, scale=-1.0),
                     waits=[(s_ac, s_ac.n)], incs=[(s_ac, 1)], tag=f"{tag}dtp2")
            plan.add("scalar",
                     lambda e: e.activation(HS_TM.bitcast(F32)[:, 0:T],
                                            HS_TM.bitcast(F32)[:, 0:T],
                                            AF.Ln, bias=C_ONE1),
                     waits=[(s_ac, s_ac.n)], incs=[(s_ac, 1)], tag=f"{tag}dtp3")
            acn = s_ac.n

            def dtp4(e):
                e.tensor_tensor(DT_SP, HS2.bitcast(F32)[:, 0:T],
                                HS_TM.bitcast(F32)[:, 0:T], ALU.add)
                e.drain()
                e.tensor_scalar(ADTF, DT_SP, C_ACOL, None, ALU.mult)
                e.drain()
                e.tensor_reduce(ASEG, ADTF, mybir.AxisListType.X, ALU.add)
                e.drain()
                return e.memset(S_RUN, 0.0)
            plan.add("vector", dtp4, waits=[(s_ac, acn)], incs=[(s_ve, 1)],
                     tag=f"{tag}dtp4")
            st[f"_{tag}prep"] = s_ve.n

            for c in range(NCHUNK):
                csl = slice(c * CH, (c + 1) * CH)
                sc_w = [(s_ve, s_ve.n)]
                if f"_{tag}acs_free{c-1}" in st:
                    sc_w.append((s_g, st[f"_{tag}acs_free{c-1}"]))

                def scan(e, csl=csl):
                    return e.tensor_tensor_scan(PR_ACS, ADTF[:, csl], ADTF[:, csl],
                                                0.0, ALU.add, ALU.bypass)
                plan.add("vector", scan, waits=sc_w, incs=[(s_ve, 1)],
                         tag=f"{tag}scan{c}")
                sc_n = s_ve.n
                dma("gpsimd", st["d_acsf"][c], PR_ACS, s_g,
                    waits=[(s_ve, sc_n)], tag=f"{tag}acssto{c}")
                st[f"_{tag}acs_free{c}"] = s_g.n
                plan.add("scalar",
                         lambda e: e.activation(PR_DS, PR_ACS, AF.Exp, scale=-1.0,
                                                bias=PR_ACS[:, 127:128]),
                         waits=[(s_ve, sc_n), (s_ac, s_ac.n)], incs=[(s_ac, 1)],
                         tag=f"{tag}ds{c}")
                plan.add("scalar",
                         lambda e, csl=csl: e.activation(PR_LN, DT_SP[:, csl], AF.Ln),
                         waits=[(s_ac, s_ac.n)], incs=[(s_ac, 1)], tag=f"{tag}lndt{c}")
                plan.add("scalar",
                         lambda e: e.activation(PR_SDO, PR_ACS, AF.Exp),
                         waits=[(s_ac, s_ac.n)], incs=[(s_ac, 1)], tag=f"{tag}sdo{c}")
                plan.add("scalar",
                         lambda e: e.activation(DECC, PR_ACS[:, 127:128], AF.Exp),
                         waits=[(s_ac, s_ac.n)], incs=[(s_ac, 1)], tag=f"{tag}dec{c}")
                ac_n = s_ac.n

                def combos(e, csl=csl):
                    e.tensor_tensor(PR_CO, PR_LN, PR_ACS, ALU.subtract)
                    e.drain()
                    return e.tensor_tensor(PR_DD, PR_DS, DT_SP[:, csl], ALU.mult)
                plan.add("vector", combos, waits=[(s_ac, ac_n)], incs=[(s_ve, 1)],
                         tag=f"{tag}co{c}")
                co_n = s_ve.n
                for k, src in enumerate((PR_CO, PR_SDO, PR_DD)):
                    plan.add("tensor",
                             lambda e, src=src: e.transpose(
                                 psb[2][:, 0:128], src, C_IDENT),
                             waits=[(s_ve, co_n)] if k == 0 else
                                   [(s_ve, s_ve.n)],
                             incs=[(s_pe, 1)], tag=f"{tag}tr{c}.{k}")
                    pe_n = s_pe.n
                    dstt = TRD if k == 2 else TRT
                    evt_w = [(s_pe, pe_n)]
                    if k > 0:
                        evt_w.append((s_g, s_g.n))

                    def evt(e, dstt=dstt):
                        return e.tensor_copy(dstt, psb[2][:, 0:128])
                    plan.add("vector", evt, waits=evt_w, incs=[(s_ve, 1)],
                             tag=f"{tag}evt{c}.{k}")
                    dma("gpsimd", st["d_prep"][c, k], dstt, s_g,
                        waits=[(s_ve, s_ve.n)], tag=f"{tag}prsto{c}.{k}")
                plan.add("tensor",
                         lambda e: e.transpose(psb[2][0:1, 128:256], DECC,
                                               C_IDENT),
                         waits=[(s_ve, s_ve.n)], incs=[(s_pe, 1)],
                         tag=f"{tag}dtr{c}")
                plan.add("vector",
                         lambda e: e.tensor_copy(DROW, psb[2][0:1, 128:256]),
                         waits=[(s_pe, s_pe.n)], incs=[(s_ve, 1)],
                         tag=f"{tag}drow{c}")
                plan.add("tensor",
                         lambda e: e.matmul(psb[2][:, 256:384], C_ONER, DROW,
                                            start=True, stop=True),
                         waits=[(s_ve, s_ve.n)], incs=[(s_pe, 1)],
                         tag=f"{tag}dbcm{c}")
                dbcev_w = [(s_pe, s_pe.n)]
                if f"_{tag}dbcst{c-1}" in st:
                    dbcev_w.append((s_g, st[f"_{tag}dbcst{c-1}"]))
                plan.add("vector",
                         lambda e: e.tensor_copy(DBC, psb[2][:, 256:384]),
                         waits=dbcev_w, incs=[(s_ve, 1)],
                         tag=f"{tag}dbc{c}")
                dbc_n = s_ve.n
                dma("gpsimd", st["d_dbc"][c], DBC, s_g, waits=[(s_ve, dbc_n)],
                    tag=f"{tag}dbcst{c}")
                st[f"_{tag}dbcst{c}"] = s_g.n
                tw = [(s_ve, st[f"_{tag}prep"])] if c == 0 else []
                if f"_{tag}hs2{c-1}" in st:
                    tw.append((s_ve, st[f"_{tag}hs2{c-1}"]))
                dma("sync", HS_TM, st["d_hs"][:, csl], s_tr, waits=tw,
                    transpose=True, tag=f"{tag}hstm{c}")
                if f"_{tag}ls_pe{c-1}" in st:
                    tw = [(s_pe, st[f"_{tag}ls_pe{c-1}"])]
                else:
                    tw = []
                dma("sync", B_TM, st["d_BC"][0:NG * NS, csl], s_tr, waits=tw,
                    transpose=True, tag=f"{tag}btm{c}")
                tr_n = s_tr.n

                def hs2f(e):
                    inst = None
                    for h in range(NH):
                        inst = e.tensor_scalar(HS2[:, h * HP:(h + 1) * HP],
                                               HS_TM[:, h * HP:(h + 1) * HP],
                                               TRD[:, h:h + 1], None, ALU.mult)
                    return inst
                hs2w = [(s_tr, tr_n), (s_ve, s_ve.n)]
                if f"_{tag}ls_pe{c-1}" in st:
                    hs2w.append((s_pe, st[f"_{tag}ls_pe{c-1}"]))
                plan.add("vector", hs2f, waits=hs2w,
                         incs=[(s_ve, 1)], tag=f"{tag}hs2{c}")
                st[f"_{tag}hs2{c}"] = s_ve.n
                hs2_n = s_ve.n
                for g in range(NG):
                    bank = 4 + (g % 2) * 2
                    mmw = [(s_tr, tr_n), (s_ve, hs2_n)] if g == 0 else []
                    if g >= 2:
                        mmw = [(s_ve, st[f"_{tag}lsev{c}.{g-2}"])]
                    for j in range(16):
                        h = g + 8 * j
                        plan.add("tensor",
                                 lambda e, g=g, j=j, h=h, bank=bank: e.matmul(
                                     psb[bank + j // 8][:, (j % 8) * 64:(j % 8) * 64 + 64],
                                     B_TM[:, g * NS:(g + 1) * NS],
                                     HS2[:, h * HP:(h + 1) * HP],
                                     start=True, stop=True),
                                 waits=(mmw if j == 0 else ()),
                                 incs=([(s_pe, 1)] if j == 15 else ()),
                                 tag=f"{tag}ls{c}.{g}.{j}")
                    st[f"_{tag}ls_pe{c}"] = s_pe.n

                    def lsev(e, g=g, bank=bank):
                        inst = None
                        for j in range(16):
                            h = g + 8 * j
                            inst = e.tensor_copy(
                                LSB[:, h * HP:(h + 1) * HP],
                                psb[bank + j // 8][:, (j % 8) * 64:(j % 8) * 64 + 64])
                        return inst
                    lsev_w = [(s_pe, s_pe.n)]
                    if g == 0 and f"_{tag}lsst{c-1}" in st:
                        lsev_w.append((s_st, st[f"_{tag}lsst{c-1}"]))
                    plan.add("vector", lsev, waits=lsev_w,
                             incs=[(s_ve, 1)], tag=f"{tag}lsev{c}.{g}")
                    st[f"_{tag}lsev{c}.{g}"] = s_ve.n
                ls_n = s_ve.n
                dma("gpsimd", st["d_ls"][c], LSB, s_st, waits=[(s_ve, ls_n)],
                    tag=f"{tag}lssto{c}")
                ls_st = s_st.n
                st[f"_{tag}lsst{c}"] = ls_st

                def supd(e):
                    inst = None
                    for h in range(NH):
                        sl = slice(h * HP, (h + 1) * HP)
                        inst = e.scalar_tensor_tensor(
                            S_RUN[:, sl], S_RUN[:, sl], DBC[:, h:h + 1],
                            LSB[:, sl], ALU.mult, ALU.add)
                    return inst
                plan.add("vector", supd, waits=[(s_ve, dbc_n)],
                         incs=[(s_ve, 1)], tag=f"{tag}supd{c}")
                st[f"_{tag}supd{c}"] = s_ve.n

        BCFM = carve(P0 + 16384, 16 * 128, BF16)
        PREPM = carve(P0 + 20480, 256)
        ACSR = carve(P0 + 21504, T, parts=(0, 1))
        ETILES = carve(P0 + 29696, 512)
        MT = carve(P0 + 31744, 512, BF16)
        EV1 = carve(P0 + 32768, 1024)
        DBCM = carve(P0 + 36864, 128)
        PREPM2 = carve(P0 + 37376, 256)
        DBCM2 = carve(P0 + 38400, 128)

        def emit_ssdmain(s0_src=None, save_stot=True, tag="sm"):
            st["_serial"] = cfg.get("smserial") or False
            if save_stot:
                dma("gpsimd", st["d_stot"][:, :], S_RUN, s_st,
                    waits=[(s_ve, s_ve.n)], tag=f"{tag}stot")
                dma("gpsimd", st["d_aseg"][:, :], ASEG, s_st, tag=f"{tag}aseg")
            sv_n = s_st.n
            sbf_w = []
            if s0_src is not None:
                dma("gpsimd", S_RUN, s0_src[:, :], s_lx, waits=[(s_st, sv_n)],
                    tag=f"{tag}seed")
                sbf_w = [(s_lx, s_lx.n)]
            else:
                sbf_w = [(s_ve, s_ve.n)]
            plan.add("vector", lambda e: e.tensor_copy(S_BF, S_RUN),
                     waits=sbf_w, incs=[(s_ve, 1)], tag=f"{tag}sbf")
            st[f"_{tag}sbf-1"] = s_ve.n

            for c in range(NCHUNK):
                csl = slice(c * CH, (c + 1) * CH)
                tw = []
                if f"_{tag}ydpe{c-1}" in st:
                    tw.append((s_pe, st[f"_{tag}ydpe{c-1}"]))
                dma("sync", HS_TM, st["d_hs"][:, csl], s_tr, waits=tw,
                    transpose=True, tag=f"{tag}hstm{c}")
                hs_n = s_tr.n
                lw = []
                if f"_{tag}supd{c-1}" in st:
                    lw.append((s_ve, st[f"_{tag}supd{c-1}"]))
                dma("gpsimd", LS_IN, st["d_ls"][c], s_d4, waits=lw,
                    tag=f"{tag}ls{c}")
                ls_n = s_d4.n
                gmw = []
                if f"_{tag}ydpe{c-1}" in st:
                    gmw.append((s_pe, st[f"_{tag}ydpe{c-1}"]))
                dma("gpsimd", BCFM.rearrange("p (k l) -> p k l", l=128),
                    st["d_BC"][:, csl].rearrange("(k p) l -> p k l", p=128),
                    s_d1, waits=gmw, tag=f"{tag}bc{c}")
                bc_n = s_d1.n
                PM = PREPM if c % 2 == 0 else PREPM2
                DM = DBCM if c % 2 == 0 else DBCM2
                pw = []
                if f"_{tag}evdone{c-2}" in st:
                    pw.append((s_ve, st[f"_{tag}evdone{c-2}"]))
                if f"_{tag}raf{c-1}" in st:
                    pw.append((s_pe, st[f"_{tag}raf{c-1}"]))
                dma("gpsimd", PM.rearrange("p (k l) -> p k l", l=128),
                    st["d_prep"][c, 0:2].rearrange("k p l -> p k l"),
                    s_d2, waits=pw, tag=f"{tag}prep{c}")
                prep_n = s_d2.n
                dbw = []
                if f"_{tag}supd{c-1}" in st:
                    dbw.append((s_ve, st[f"_{tag}supd{c-1}"]))
                dma("gpsimd", DM, st["d_dbc"][c], s_d5, waits=dbw,
                    tag=f"{tag}dbc{c}")
                dbc_n = s_d5.n

                for g in range(NG):
                    aw = []
                    if f"_{tag}rapel{c}.{g-1}" in st:
                        aw.append((s_pe, st[f"_{tag}rapel{c}.{g-1}"]))
                    elif f"_{tag}rapel{c-1}.{NG-1}" in st:
                        aw.append((s_pe, st[f"_{tag}rapel{c-1}.{NG-1}"]))
                    dma("gpsimd", ACSR.rearrange("o (j l) -> o j l", l=128),
                        st["d_acsf"][c].rearrange("(j g) l -> g j l", g=NG)[g],
                        s_d3, waits=aw, tag=f"{tag}acsr{c}.{g}")
                    acsr_n = s_d3.n
                    gbank = 1 + (g % 2)
                    gslot = (g // 2) % 2
                    gm_w = [(s_d1, bc_n)]
                    if f"_{tag}mpp{c}.{g-2}" in st:
                        gm_w.append((s_ve, st[f"_{tag}mpp{c}.{g-2}"]))
                    elif c > 0 and f"_{tag}mpp{c-1}.{g+6}" in st:
                        gm_w.append((s_ve, st[f"_{tag}mpp{c-1}.{g+6}"]))
                    plan.add("tensor",
                             lambda e, g=g, gbank=gbank, gslot=gslot: e.matmul(
                                 psb[gbank][:, gslot * 128:(gslot + 1) * 128],
                                 BCFM[:, g * 128:(g + 1) * 128],
                                 BCFM[:, (8 + g) * 128:(9 + g) * 128],
                                 start=True, stop=True),
                             waits=gm_w, incs=[(s_pe, 1)], tag=f"{tag}gm{c}.{g}")
                    st[f"_{tag}gmpe{c}"] = s_pe.n
                    gm_n = s_pe.n
                    yo_w = [(s_d1, bc_n), (s_ve, st[f"_{tag}sbf{c-1}"]
                                           if c > 0 else st[f"_{tag}sbf-1"])]
                    if f"_{tag}evdone{c}.{g-1}" in st:
                        yo_w.append((s_ve, st[f"_{tag}evdone{c}.{g-1}"]))
                    elif c > 0 and f"_{tag}evdone{c-1}.{NG-1}" in st:
                        yo_w.append((s_ve, st[f"_{tag}evdone{c-1}.{NG-1}"]))
                    for j in range(16):
                        h = g + 8 * j
                        plan.add("tensor",
                                 lambda e, g=g, j=j, h=h: e.matmul(
                                     psb[4 + j // 8][:, (j % 8) * 64:(j % 8) * 64 + 64],
                                     BCFM[:, (8 + g) * 128:(9 + g) * 128],
                                     S_BF[:, h * HP:(h + 1) * HP],
                                     start=True, stop=True),
                                 waits=(yo_w if j == 0 else ()),
                                 incs=([(s_pe, 1)] if j == 15 else ()),
                                 tag=f"{tag}yo{c}.{g}.{j}")
                    yo_n = s_pe.n
                    for jb in range(4):
                        for jj in range(4):
                            j = jb * 4 + jj
                            h = g + 8 * j
                            ra_w = [(s_d3, acsr_n), (s_d2, prep_n)]
                            if f"_{tag}epn{c}.{g}.{j-2}" in st:
                                ra_w.append((s_ac, st[f"_{tag}epn{c}.{g}.{j-2}"]))
                            elif j < 2 and f"_{tag}eac_last" in st:
                                ra_w.append((s_ac, st[f"_{tag}eac_last"]))
                            if f"_{tag}mplast" in st:
                                ra_w.append((s_ve, st[f"_{tag}mplast"]))
                            rbank = 3 * (j % 2)
                            plan.add("tensor",
                                     lambda e, j=j, rbank=rbank: e.matmul(
                                         psb[rbank][:, 0:128], C_ONER,
                                         ACSR[0:1, j * 128:(j + 1) * 128],
                                         start=True, stop=False),
                                     waits=ra_w, tag=f"{tag}ra{c}.{g}.{j}")
                            plan.add("tensor",
                                     lambda e, rbank=rbank: e.matmul(
                                         psb[rbank][:, 0:128], C_AM, C_BM,
                                         start=False, stop=True),
                                     incs=[(s_pe, 1)], tag=f"{tag}rb{c}.{g}.{j}")
                            st[f"_{tag}rapel{c}.{g}"] = s_pe.n
                            if g == 0 and j == 0:
                                st[f"_{tag}raf{c}"] = s_pe.n
                            ra_n = s_pe.n
                            e_w = [(s_pe, ra_n)]
                            plan.add("scalar",
                                     lambda e, jj=jj, h=h, rbank=rbank, PM=PM: e.activation(
                                         ETILES[:, jj * 128:(jj + 1) * 128],
                                         psb[rbank][:, 0:128],
                                         AF.Exp, bias=PM[:, h:h + 1]),
                                     waits=e_w, incs=[(s_ac, 1)],
                                     tag=f"{tag}ep{c}.{g}.{j}")
                            st[f"_{tag}epn{c}.{g}.{j}"] = s_ac.n
                            ep_n = s_ac.n
                            mw = [(s_ac, ep_n), (s_pe, gm_n)]
                            if f"_{tag}ydpe_s{c}.{g}.{j-4}" in st:
                                mw.append((s_pe, st[f"_{tag}ydpe_s{c}.{g}.{j-4}"]))
                            plan.add("vector",
                                     lambda e, jj=jj, gbank=gbank, gslot=gslot:
                                     e.tensor_tensor(
                                         MT[:, jj * 128:(jj + 1) * 128],
                                         psb[gbank][:, gslot * 128:(gslot + 1) * 128],
                                         ETILES[:, jj * 128:(jj + 1) * 128],
                                         ALU.mult),
                                     waits=mw, incs=[(s_ve, 1)],
                                     tag=f"{tag}mp{c}.{g}.{j}")
                            st[f"_{tag}mpp{c}.{g}"] = s_ve.n
                            st[f"_{tag}mplast"] = s_ve.n
                            mp_n = s_ve.n
                            yd_w = [(s_ve, mp_n), (s_tr, hs_n)]
                            if c > 0 and g == 0 and j == 0:
                                yd_w.append((s_ve, st[f"_{tag}ev2{c-1}.{NG-1}"]))
                            if g > 0 and j == 0 and jb == 0:
                                yd_w.append((s_ve, st[f"_{tag}ev2{c}.{g-1}"]))
                            plan.add("tensor",
                                     lambda e, jj=jj, j=j, h=h: e.matmul(
                                         psb[6 + j // 8][:, (j % 8) * 64:(j % 8) * 64 + 64],
                                         MT[:, jj * 128:(jj + 1) * 128],
                                         HS_TM[:, h * HP:(h + 1) * HP],
                                         start=True, stop=True),
                                     waits=yd_w,
                                     incs=[(s_pe, 1)], tag=f"{tag}yd{c}.{g}.{j}")
                            st[f"_{tag}ydpe_s{c}.{g}.{j}"] = s_pe.n
                            st[f"_{tag}ydpe{c}"] = s_pe.n
                        st[f"_{tag}eac{c}.{g}.{jb}"] = s_ac.n
                        st[f"_{tag}eac_last"] = s_ac.n
                    yd_n = s_pe.n
                    for j in range(16):
                        h = g + 8 * j
                        e1_w = [(s_pe, yd_n)] if j == 0 else [(s_ac, s_ac.n)]
                        if j == 0 and f"_{tag}ev2{c}.{g-1}" in st:
                            e1_w.append((s_ve, st[f"_{tag}ev2{c}.{g-1}"]))
                        if j == 0 and c > 0 and g == 0:
                            e1_w.append((s_ve, st[f"_{tag}ev2{c-1}.{NG-1}"]))
                        plan.add("scalar",
                                 lambda e, j=j, h=h, PM=PM: e.activation(
                                     EV1[:, j * 64:(j + 1) * 64],
                                     psb[4 + j // 8][:, (j % 8) * 64:(j % 8) * 64 + 64],
                                     AF.Identity, scale=PM[:, 128 + h:129 + h]),
                                 waits=e1_w, incs=[(s_ac, 1)],
                                 tag=f"{tag}e1{c}.{g}.{j}")
                    e1_n = s_ac.n

                    def ev2(e, g=g):
                        v = YC.rearrange("p (j g2 r) -> p j g2 r", g2=NG, r=HP)
                        e.tensor_tensor(v[:, 0:8, g, :], EV1.rearrange(
                            "p (j r) -> p j r", r=HP)[:, 0:8, :],
                            psb[6].rearrange("p (j r) -> p j r", r=HP)[:, 0:8, :],
                            ALU.add)
                        e.drain()
                        return e.tensor_tensor(v[:, 8:16, g, :], EV1.rearrange(
                            "p (j r) -> p j r", r=HP)[:, 8:16, :],
                            psb[7].rearrange("p (j r) -> p j r", r=HP)[:, 0:8, :],
                            ALU.add)
                    ev2_w = [(s_ac, e1_n)]
                    if g == 0 and f"_{tag}yst{c-1}" in st:
                        ev2_w.append((s_st, st[f"_{tag}yst{c-1}"]))
                    plan.add("vector", ev2, waits=ev2_w, incs=[(s_ve, 1)],
                             tag=f"{tag}ev2{c}.{g}")
                    st[f"_{tag}ev2{c}.{g}"] = s_ve.n
                    st[f"_{tag}evdone{c}.{g}"] = s_ve.n
                    st[f"_{tag}evdone{c}"] = s_ve.n

                yst_w = [(s_ve, st[f"_{tag}ev2{c}.{NG-1}"])]
                dma("gpsimd", st["d_y"][csl, :], YC, s_st, waits=yst_w,
                    tag=f"{tag}ysto{c}")
                st[f"_{tag}yst{c}"] = s_st.n

                def supd(e, DM=DM):
                    inst = None
                    for h in range(NH):
                        sl = slice(h * HP, (h + 1) * HP)
                        inst = e.scalar_tensor_tensor(
                            S_RUN[:, sl], S_RUN[:, sl], DM[:, h:h + 1],
                            LS_IN[:, sl], ALU.mult, ALU.add)
                    return inst
                plan.add("vector", supd,
                         waits=[(s_d4, ls_n), (s_d5, dbc_n)],
                         incs=[(s_ve, 1)], tag=f"{tag}supd{c}")
                st[f"_{tag}supd{c}"] = s_ve.n

                def sbf(e):
                    return e.tensor_copy(S_BF, S_RUN)
                plan.add("vector", sbf, incs=[(s_ve, 1)], tag=f"{tag}sbf{c}")
                st[f"_{tag}sbf{c}"] = s_ve.n
            st["_serial"] = False

        GSQ = carve(P0, T)
        SGT = carve(P0 + 8192, T, BF16)
        XG1 = carve(P0 + 12288, T)
        NT = carve(P0 + 20480, T, BF16)
        YT = carve(P0 + 24576, T, BF16)
        GATET = carve(P0 + 28672, T, BF16)
        HST = carve(P0 + 32768, T, BF16)
        RBC = carve(P0 + 36864, T)
        XGB = [carve(RTOP + 32768 + 8192 * i, T) for i in range(8)]

        def emit_gating(tag="gt"):
            for gq in range(NG):
                for f8 in range(8):
                    fi = gq * 8 + f8
                    w = []
                    if f"_{tag}sg_used{fi-1}" in st:
                        w.append((s_ve, st[f"_{tag}sg_used{fi-1}"]))
                    dma("sync", YT, st["d_y"][:, fi * 128:(fi + 1) * 128],
                        s_tr, waits=list(w), transpose=True, tag=f"{tag}y{fi}")
                    y_n = s_tr.n
                    dma("gpsimd", GATET, st["d_gate"][fi * 128:(fi + 1) * 128, :],
                        s_lx, waits=list(w), tag=f"{tag}g{fi}")
                    dma("gpsimd", HST, st["d_hs"][fi * 128:(fi + 1) * 128, :],
                        s_lx, tag=f"{tag}h{fi}")
                    h_n = s_lx.n
                    si_w = [(s_lx, h_n)]
                    if f"_{tag}sg_used{fi-1}" in st:
                        si_w.append((s_ve, st[f"_{tag}sg_used{fi-1}"]))
                    plan.add("scalar",
                             lambda e: e.activation(SGT, GATET, AF.Silu),
                             waits=si_w, incs=[(s_ac, 1)], tag=f"{tag}si{fi}")
                    sg_n = s_ac.n

                    def xgf(e, gq=gq, f8=f8, fi=fi):
                        e.scalar_tensor_tensor(XG1, HST, C_DREP[:, fi:fi + 1], YT,
                                               ALU.mult, ALU.add)
                        e.drain()
                        return e.tensor_tensor(XGB[f8], XG1, SGT, ALU.mult)
                    plan.add("vector", xgf,
                             waits=[(s_tr, y_n), (s_lx, h_n), (s_ac, sg_n)],
                             incs=[(s_ve, 1)], tag=f"{tag}xg{gq}.{f8}")
                    st[f"_{tag}xg{gq}.{f8}"] = s_ve.n
                    st[f"_{tag}sg_used{fi}"] = s_ve.n
                    sq_w = [(s_ve, s_ve.n)]
                    if f"_{tag}qpe{gq}.{f8-1}" in st:
                        sq_w.append((s_pe, st[f"_{tag}qpe{gq}.{f8-1}"]))
                    elif f"_{tag}bcpe{gq-1}" in st:
                        sq_w.append((s_pe, st[f"_{tag}bcpe{gq-1}"]))
                    plan.add("scalar",
                             lambda e, f8=f8: e.activation(GSQ, XGB[f8], AF.Square),
                             waits=sq_w, incs=[(s_ac, 1)], tag=f"{tag}sq{fi}")
                    q_n = s_ac.n
                    for tb in range(NTB):
                        plan.add("tensor",
                                 lambda e, tb=tb, f8=f8: e.matmul(
                                     psb[tb][0:1, :], C_ONEC,
                                     GSQ[:, tb * TB:(tb + 1) * TB],
                                     start=(f8 == 0), stop=(f8 == 7)),
                                 waits=[(s_ac, q_n)] if tb == 0 else (),
                                 incs=[(s_pe, 1)] if tb == NTB - 1 else (),
                                 tag=f"{tag}q{fi}.{tb}")
                    st[f"_{tag}qpe{gq}.{f8}"] = s_pe.n
                q_pe = s_pe.n

                def rws(e):
                    inst = None
                    for tb in range(NTB):
                        inst = e.tensor_scalar(ROWQ[:, tb * TB:(tb + 1) * TB],
                                               psb[tb][0:1, :],
                                               1.0 / (INTER // NG), None, ALU.mult)
                    return inst
                plan.add("vector", rws, waits=[(s_pe, q_pe)], incs=[(s_ve, 1)],
                         tag=f"{tag}rws{gq}")
                plan.add("scalar",
                         lambda e: e.activation(ROWT, ROWQ, AF.Ln, bias=C_EPS[0:1, :]),
                         waits=[(s_ve, s_ve.n)], incs=[(s_ac, 1)], tag=f"{tag}ln{gq}")
                plan.add("scalar",
                         lambda e: e.activation(ROWA, ROWT, AF.Exp, scale=-0.5),
                         waits=[(s_ac, s_ac.n)], incs=[(s_ac, 1)], tag=f"{tag}ex{gq}")
                r_n = s_ac.n
                for tb in range(NTB):
                    plan.add("tensor",
                             lambda e, tb=tb: e.matmul(
                                 psb[4 + tb][:, :], C_ONER,
                                 ROWA[:, tb * TB:(tb + 1) * TB],
                                 start=True, stop=True),
                             waits=[(s_ac, r_n)] if tb == 0 else (),
                             incs=[(s_pe, 1)] if tb == NTB - 1 else (),
                             tag=f"{tag}bc{gq}.{tb}")
                st[f"_{tag}bcpe{gq}"] = s_pe.n

                def rbcp(e):
                    inst = None
                    for tb in range(NTB):
                        inst = e.tensor_copy(RBC[:, tb * TB:(tb + 1) * TB],
                                             psb[4 + tb][:, :])
                    return inst
                plan.add("vector", rbcp, waits=[(s_pe, s_pe.n)], incs=[(s_ve, 1)],
                         tag=f"{tag}rbc{gq}")
                rbc_n = s_ve.n
                for f8 in range(8):
                    fi = gq * 8 + f8
                    nw_w = [(s_ve, rbc_n)]
                    if f"_{tag}nst{fi-1}" in st:
                        nw_w.append((s_st, st[f"_{tag}nst{fi-1}"]))

                    def nrm(e, f8=f8, fi=fi):
                        return e.scalar_tensor_tensor(
                            NT, XGB[f8], C_NW[:, fi:fi + 1], RBC, ALU.mult, ALU.mult)
                    plan.add("vector", nrm, waits=nw_w, incs=[(s_ve, 1)],
                             tag=f"{tag}nrm{fi}")
                    dma("gpsimd", st["d_normed"][fi * 128:(fi + 1) * 128, :], NT,
                        s_st, waits=[(s_ve, s_ve.n)], tag=f"{tag}nst{fi}")
                    st[f"_{tag}nst{fi}"] = s_st.n

        WOB0 = carve(P0 + 16384, INTER, BF16)
        WOB1 = carve(P0 + 32768, INTER, BF16)
        NRES = carve(RTOP, IFC * 1024, BF16)
        TH = T // 2

        def emit_outproj(tag="op"):
            for th in range(2):
                tsl = slice(th * TH, (th + 1) * TH)
                ttag = f"{tag}{th}"
                nw_ = [(s_pe, st[f"_{tag}0pe{DFC-1}"])] if th == 1 else []
                dma("gpsimd", NRES.rearrange("p (k l) -> p k l", l=1024),
                    st["d_normed"][:, tsl].rearrange("(k p) l -> p k l", p=128),
                    s_lx, waits=nw_, tag=f"{ttag}nres")
                nres_n = s_lx.n
                for oc in range(DFC):
                    wb = WOB0 if oc % 2 == 0 else WOB1
                    w = []
                    if oc >= 2:
                        w.append((s_pe, st[f"_{ttag}pe{oc-2}"]))
                        w.append((s_ve, st[f"_{ttag}ev{oc-2}"]))
                    elif th == 1 and oc < 2:
                        w.append((s_pe, st[f"_{tag}0pe{DFC-2+oc}"]))
                        w.append((s_ve, st[f"_{tag}0ev{DFC-2+oc}"]))
                    dma("gpsimd", wb.rearrange("p (fc o) -> p fc o", o=128),
                        st["woutT"][:, oc * 128:(oc + 1) * 128].rearrange(
                            "(fc p) o -> p fc o", p=128),
                        s_lw, waits=w, tag=f"{ttag}lw{oc}")
                    mm_w = [(s_lw, s_lw.n), (s_lx, nres_n)]
                    if oc >= 2:
                        mm_w.append((s_ve, st[f"_{ttag}ev{oc-2}"]))
                    grp = (oc % 2) * 2
                    for fc in range(IFC):
                        for tb in range(2):
                            plan.add("tensor",
                                     lambda e, wb=wb, fc=fc, tb=tb, grp=grp: e.matmul(
                                         psb[grp + tb][:, :],
                                         wb[:, fc * 128:(fc + 1) * 128],
                                         NRES[:, fc * 1024 + tb * TB:
                                              fc * 1024 + (tb + 1) * TB],
                                         start=(fc == 0), stop=(fc == IFC - 1)),
                                     waits=(mm_w if (fc == 0 and tb == 0) else ()),
                                     incs=([(s_pe, 1)] if (fc == IFC - 1 and tb == 1)
                                           else ()),
                                     tag=f"{ttag}mm{oc}.{fc}.{tb}")
                    st[f"_{ttag}pe{oc}"] = s_pe.n
                    pen = s_pe.n
                    xw = []
                    if oc >= 2:
                        xw.append((s_ve, st[f"_{ttag}ev{oc-2}"]))
                    dma("gpsimd", XA[:, 0:TH],
                        st["d_x1"][oc * 128:(oc + 1) * 128, tsl], s_lx,
                        waits=xw, tag=f"{ttag}lx{oc}")
                    lx_n = s_lx.n
                    ev_w = [(s_pe, pen), (s_lx, lx_n)]
                    if f"_{ttag}xst{oc-1}" in st:
                        ev_w.append((s_st, st[f"_{ttag}xst{oc-1}"]))

                    def ev(e, grp=grp):
                        e.tensor_tensor(T1[:, 0:TB], psb[grp][:, :],
                                        XA[:, 0:TB], ALU.add)
                        e.drain()
                        return e.tensor_tensor(T1[:, TB:TH], psb[grp + 1][:, :],
                                               XA[:, TB:TH], ALU.add)
                    plan.add("vector", ev, waits=ev_w, incs=[(s_ve, 1)],
                             tag=f"{ttag}ev{oc}")
                    st[f"_{ttag}ev{oc}"] = s_ve.n
                    dma("gpsimd", st["d_x2"][oc * 128:(oc + 1) * 128, tsl],
                        T1[:, 0:TH], s_st, waits=[(s_ve, s_ve.n)],
                        tag=f"{ttag}xst{oc}")
                    st[f"_{ttag}xst{oc}"] = s_st.n

        HALO_BF = carve(P0 + 37376, 3, BF16)

        def emit_convonly(halo_src_bf, tag="cvo"):
            for j in range(CONVD // 128):
                w = []
                if f"_{tag}cv{j-1}" in st:
                    w.append((s_ve, st[f"_{tag}cv{j-1}"]))
                dma("gpsimd", HPRE[:, 3:3 + T],
                    st["d_hbcpre"][j * 128:(j + 1) * 128, :], s_lx, waits=w,
                    tag=f"{tag}ld{j}")
                dma("gpsimd", HPRE[:, 0:3],
                    halo_src_bf[j * 128:(j + 1) * 128, :], s_lx, tag=f"{tag}hl{j}")
                ldn = s_lx.n

                def conv(e, j=j):
                    e.tensor_scalar(CT1, HPRE[:, 0:T], C_CW[:, 4 * j:4 * j + 1],
                                    None, ALU.mult)
                    e.drain()
                    e.scalar_tensor_tensor(CT2, HPRE[:, 1:1 + T],
                                           C_CW[:, 4 * j + 1:4 * j + 2], CT1,
                                           ALU.mult, ALU.add)
                    e.drain()
                    e.scalar_tensor_tensor(CT1, HPRE[:, 2:2 + T],
                                           C_CW[:, 4 * j + 2:4 * j + 3], CT2,
                                           ALU.mult, ALU.add)
                    e.drain()
                    return e.scalar_tensor_tensor(CT2, HPRE[:, 3:3 + T],
                                                  C_CW[:, 4 * j + 3:4 * j + 4], CT1,
                                                  ALU.mult, ALU.add)
                cw_w = [(s_lx, ldn)]
                if f"_{tag}si{j-1}" in st:
                    cw_w.append((s_ac, st[f"_{tag}si{j-1}"]))
                plan.add("vector", conv, waits=cw_w, incs=[(s_ve, 1)],
                         tag=f"{tag}cv{j}")
                st[f"_{tag}cv{j}"] = s_ve.n
                si_w = [(s_ve, s_ve.n)]
                if f"_{tag}st{j-1}" in st:
                    si_w.append((s_st, st[f"_{tag}st{j-1}"]))
                plan.add("scalar",
                         lambda e, j=j: e.activation(
                             OUTT, CT2, AF.Silu, bias=C_CB[:, j:j + 1]),
                         waits=si_w, incs=[(s_ac, 1)], tag=f"{tag}si{j}")
                st[f"_{tag}si{j}"] = s_ac.n
                if j < 64:
                    dst = st["d_hs"][j * 128:(j + 1) * 128, :]
                else:
                    dst = st["d_BC"][(j - 64) * 128:(j - 63) * 128, :]
                dma("gpsimd", dst, OUTT, s_st, waits=[(s_ac, s_ac.n)],
                    tag=f"{tag}sto{j}")
                st[f"_{tag}st{j}"] = s_st.n

        def emit_seedcombine(ps_names, pa_names, tag="sc"):
            STG = carve(RTOP + 32768, INTER)
            FCOL = carve(P0 + 37408, 1)
            FROW = carve(P0 + 37440, 128, parts=(0, 1))
            FBC = carve(P0 + 38016, 128)
            plan.add("vector", lambda e: e.memset(S_RUN, 0.0),
                     waits=[(s_ve, s_ve.n), (s_st, s_st.n)],
                     incs=[(s_ve, 1)], tag=f"{tag}z")
            for j in range(3):
                dma("gpsimd", STG, st[ps_names[j]][:, :], s_lx,
                    waits=[(s_ve, st[f"_{tag}done{j-1}"])] if j > 0 else
                    [(s_ve, s_ve.n)], tag=f"{tag}ld{j}")
                dma("gpsimd", FCOL, st[pa_names[j]][:, :], s_lx, tag=f"{tag}la{j}")
                ld_n = s_lx.n
                plan.add("scalar",
                         lambda e, j=j: e.activation(
                             FCOL, FCOL, AF.Exp,
                             scale=st["_smk"][:, 3 + j:4 + j]),
                         waits=[(s_lx, ld_n), (s_ac, s_ac.n)], incs=[(s_ac, 1)],
                         tag=f"{tag}fe{j}")
                plan.add("tensor",
                         lambda e: e.transpose(psb[2][0:1, 0:128], FCOL, C_IDENT),
                         waits=[(s_ac, s_ac.n)], incs=[(s_pe, 1)], tag=f"{tag}tr{j}")
                plan.add("vector",
                         lambda e: e.tensor_copy(FROW, psb[2][0:1, 0:128]),
                         waits=[(s_pe, s_pe.n)], incs=[(s_ve, 1)], tag=f"{tag}rw{j}")
                plan.add("tensor",
                         lambda e: e.matmul(psb[2][:, 128:256], C_ONER, FROW,
                                            start=True, stop=True),
                         waits=[(s_ve, s_ve.n)], incs=[(s_pe, 1)], tag=f"{tag}bm{j}")
                plan.add("vector",
                         lambda e: e.tensor_copy(FBC, psb[2][:, 128:256]),
                         waits=[(s_pe, s_pe.n)], incs=[(s_ve, 1)], tag=f"{tag}fb{j}")
                plan.add("vector",
                         lambda e, j=j: e.tensor_scalar(
                             STG, STG, st["_smk"][:, j:j + 1], None, ALU.mult),
                         waits=[(s_lx, ld_n)], incs=[(s_ve, 1)], tag=f"{tag}ms{j}")

                def fold(e):
                    inst = None
                    for h in range(NH):
                        sl = slice(h * HP, (h + 1) * HP)
                        inst = e.scalar_tensor_tensor(
                            S_RUN[:, sl], S_RUN[:, sl], FBC[:, h:h + 1],
                            STG[:, sl], ALU.mult, ALU.add)
                    return inst
                plan.add("vector", fold, incs=[(s_ve, 1)], tag=f"{tag}fold{j}")
                st[f"_{tag}done{j}"] = s_ve.n

        stages = cfg.get("stages", ())

        def stg_dtdump():
            dma("gpsimd", st["d_dt"][:, :], DT_SP, s_st, waits=[(s_ve, s_ve.n)])

        def stg_dtload():
            dma("gpsimd", DT_SP, st["d_dt"][:, :], s_lx)

        def stg_stotdump():
            dma("gpsimd", st["d_stot"][:, :], S_RUN, s_st, waits=[(s_ve, s_ve.n)])
            dma("gpsimd", st["d_aseg"][:, :], ASEG, s_st)

        def stg_dbgprep():
            od = dout("dbg_adtf", [128, T])
            dma("gpsimd", od[:, :], ADTF, s_st, waits=[(s_ve, st["_saprep"])])
            od2 = dout("dbg_aseg2", [128, 1])
            dma("gpsimd", od2[:, :], ASEG, s_st)
            od3 = dout("dbg_dtsp", [128, T])
            dma("gpsimd", od3[:, :], DT_SP, s_st)

        STAGE_EMIT = {
            "ln0": lambda: emit_layernorm(st["xT"], True, "0"),
            "f1": lambda: emit_ffn(st["xT"], st["d_x1"], "1", cache_wb=st["d_wbT"]),
            "ln1": lambda: emit_layernorm(st["d_x1"], False, "1"),
            "i1": lambda: emit_inproj(st["d_halo"]),
            "i1nc": lambda: emit_inproj(st["d_halo"], do_conv=False),
            "dtdump": stg_dtdump,
            "dtload": stg_dtload,
            "convonly": lambda: emit_convonly(st["halo_bf"]),
            "ssda": emit_ssda,
            "stotdump": stg_stotdump,
            "dbgprep": stg_dbgprep,
            "seedcombine": lambda: emit_seedcombine(
                [f"d_ps{j}" for j in range(3)], [f"d_pa{j}" for j in range(3)]),
            "ssdmain": lambda: emit_ssdmain(st["d_S0"]),
            "ssdmain_seeded": lambda: emit_ssdmain(None, save_stot=False),
            "gating": emit_gating,
            "outproj": emit_outproj,
            "ln2": lambda: emit_layernorm(st["d_x2"], True, "2"),
            "f2": lambda: emit_ffn(st["d_x2"], st["outT"], "3", use_cached=True),
            "final_dummy": lambda: plan.add(
                "gpsimd", lambda e: e.dma_start(out=st["outT"][:, :],
                                                in_=st["d_x1"][:, :]),
                incs=[(s_st, 16)], tag="outdummy"),
        }
        for stg in stages:
            STAGE_EMIT[stg]()
            barrier(stg)

        for name in dbg:
            if name == "seed":
                od = dout("dbg_seed", [NS, INTER])
                dma("gpsimd", od[:, :], S_RUN, s_st, waits=[(s_ve, s_ve.n)])
            if name == "tail":
                od = dout("dbg_normed", [INTER, T], BF16)
                plan.add("gpsimd", lambda e, od=od: e.dma_start(
                    out=od[:, :], in_=st["d_normed"][:, :]), incs=[(s_st, 16)])
                od2 = dout("dbg_x2", [D, T])
                plan.add("gpsimd", lambda e, od2=od2: e.dma_start(
                    out=od2[:, :], in_=st["d_x2"][:, :]), incs=[(s_st, 16)])
            if name == "ssdm":
                od = dout("dbg_y", [T, INTER], BF16)
                plan.add("gpsimd", lambda e, od=od: e.dma_start(
                    out=od[:, :], in_=st["d_y"][:, :]), incs=[(s_st, 16)])
                od2 = dout("dbg_stot", [NS, INTER])
                plan.add("gpsimd", lambda e, od2=od2: e.dma_start(
                    out=od2[:, :], in_=st["d_stot"][:, :]), incs=[(s_st, 16)])
                od3 = dout("dbg_aseg", [128, 1])
                plan.add("gpsimd", lambda e, od3=od3: e.dma_start(
                    out=od3[:, :], in_=st["d_aseg"][:, :]), incs=[(s_st, 16)])
            if name == "ssda":
                od = dout("dbg_ls", [NCHUNK, NS, INTER], BF16)
                plan.add("gpsimd", lambda e, od=od: e.dma_start(
                    out=od[:, :, :], in_=st["d_ls"][:, :, :]), incs=[(s_st, 16)])
                if "ssdm" not in dbg:
                    od2 = dout("dbg_stot", [NS, INTER])
                    dma("gpsimd", od2[:, :], S_RUN, s_st)
                    od3 = dout("dbg_aseg", [128, 1])
                    dma("gpsimd", od3[:, :], ASEG, s_st)
                od4 = dout("dbg_acs", [NCHUNK, 128, 128])
                plan.add("gpsimd", lambda e, od4=od4: e.dma_start(
                    out=od4[:, :, :], in_=st["d_acsf"][:, :, :]), incs=[(s_st, 16)])
                od5 = dout("dbg_prep", [NCHUNK, 3, 128, 128])
                plan.add("gpsimd", lambda e, od5=od5: e.dma_start(
                    out=od5[:, :, :, :], in_=st["d_prep"][:, :, :, :]), incs=[(s_st, 16)])
            if name == "mixin":
                od = dout("dbg_gate", [INTER, T], BF16)
                plan.add("gpsimd", lambda e, od=od: e.dma_start(
                    out=od[:, :], in_=st["d_gate"][:, :]), incs=[(s_st, 16)])
                od2 = dout("dbg_hs", [INTER, T], BF16)
                plan.add("gpsimd", lambda e, od2=od2: e.dma_start(
                    out=od2[:, :], in_=st["d_hs"][:, :]), incs=[(s_st, 16)])
                od3 = dout("dbg_BC", [2 * NG * NS, T], BF16)
                plan.add("gpsimd", lambda e, od3=od3: e.dma_start(
                    out=od3[:, :], in_=st["d_BC"][:, :]), incs=[(s_st, 16)])
                od4 = dout("dbg_dt", [128, T])
                dma("gpsimd", od4[:, :], DT_SP, s_st)
            if name == "lnrows":
                od = dout("dbg_rows", [4, T])
                dma("gpsimd", od[0:1, :], ROWM, s_st, tag="dbgr0")
                dma("gpsimd", od[1:2, :], ROWA, s_st, tag="dbgr1")
                od2 = dout("dbg_ab", [2, 128, T], BF16)
                dma("gpsimd", od2[0].bitcast(F32) if False else od2[0], ABC.bitcast(F32)[:, 0:T] if False else ABC, s_st, tag="dbgr2")
                dma("gpsimd", od2[1], BBC, s_st, tag="dbgr3")
            if name == "xln":
                od = dout("dbg_xln", [D, T], BF16)
                for fc in range(DFC):
                    dma("gpsimd", od[fc * 128:(fc + 1) * 128, :],
                        ACT[:, fc * T:(fc + 1) * T], s_st, tag=f"dbg{fc}")
            if name == "x1":
                od = dout("dbg_x1", [D, T])
                plan.add("gpsimd", lambda e, od=od: e.dma_start(
                    out=od[:, :], in_=st["d_x1"][:, :]), incs=[(s_st, 16)], tag="dbgx1")

        if "final_dummy" in stages:
            plan.add("gpsimd", lambda e: e.dma_start(
                out=st["outT"][:, :], in_=st["d_x1"][:, :]), incs=[(s_st, 16)],
                tag="outdummy")

        barrier("end")

        blk = ctx.enter_context(nc.Block())

        def mk_body(ename):
            def body(eng):
                for waits, fn, incs, tag in plan.ops[ename]:
                    for s, v in waits:
                        eng.wait_ge(s.h, v)
                    inst = fn(eng)
                    for s, k in incs:
                        assert inst is not None, f"inc on None inst: {tag}"
                        inst.then_inc(s.h, k)
            return body

        blk.sync(mk_body("sync"))
        blk.tensor(mk_body("tensor"))
        blk.vector(mk_body("vector"))
        blk.scalar(mk_body("scalar"))
        blk.gpsimd(mk_body("gpsimd"))

    return nc, st




def make_in_maps(inputs, cfg):
    x = inputs["L0_attn_hidden_states"]
    winT = np.ascontiguousarray(inputs["L0_attn_in_proj_w"].T)
    wfT = np.ascontiguousarray(inputs["L0_ffn_weight"].T)
    woutT = np.ascontiguousarray(inputs["L0_attn_out_proj_w"].T)
    bias_fm = np.ascontiguousarray(inputs["L0_ffn_bias"].reshape(DFC, 128).T)
    conv_w = np.ascontiguousarray(
        inputs["L0_attn_conv_w"][:, 0, :].reshape(CONVD // 128, 128, KC)
        .transpose(1, 0, 2).reshape(128, (CONVD // 128) * KC))
    conv_b_fm = np.ascontiguousarray(
        inputs["L0_attn_conv_b"].reshape(CONVD // 128, 128).T)
    dt_bias_col = np.ascontiguousarray(inputs["L0_attn_dt_bias"][:, None])
    A_log_col = np.ascontiguousarray(inputs["L0_attn_A_log"][:, None])
    D_rep_fm = np.ascontiguousarray(
        np.repeat(inputs["L0_attn_D"], HP).reshape(INTER // 128, 128).T)
    nw_fm = np.ascontiguousarray(
        inputs["L0_attn_norm_weight"].reshape(INTER // 128, 128).T)
    kk = np.asarray(inputs["L0_ffn_kk"], np.float32)
    aa = np.asarray(inputs["L0_ffn_aa"], np.float32)
    kk_col = np.broadcast_to(kk, (128, 1)).astype(np.float32).copy()
    aa11 = np.broadcast_to(aa, (1, 1)).astype(np.float32).copy()
    k = np.arange(128)
    amask = (k[:, None] < k[None, :]).astype(BFNP)
    bmask = ((k[None, :] <= k[:, None]).astype(np.float32) * NEGBIG).astype(BFNP)
    bmask4 = np.ascontiguousarray(np.tile(bmask, (1, 4)))
    ones_col = np.ones((128, 1), np.float32)
    ones_row = np.ones((1, 128), np.float32)
    ident = np.eye(128, dtype=np.float32)

    in_maps = []
    for core in range(8):
        b, seg = divmod(core, 4)
        tok0 = seg * T
        xT = np.ascontiguousarray(x[b, tok0:tok0 + T, :].T)
        m = dict(xT=xT, winT=winT, wfT=wfT, woutT=woutT,
                 ffn_bias_fm=bias_fm, conv_w=conv_w, conv_b_fm=conv_b_fm,
                 dt_bias_col=dt_bias_col, A_log_col=A_log_col,
                 D_rep_fm=D_rep_fm, nw_fm=nw_fm, kk_col=kk_col, aa11=aa11,
                 ones_col=ones_col, ones_row=ones_row, ident=ident,
                 amask=amask, bmask=bmask, bmask4=bmask4,
                 d_halo=np.zeros((CONVD, 3), np.float32),
                 d_S0=np.zeros((NS, INTER), np.float32),
                 halo_bf=np.zeros((CONVD, 3), BFNP),
                 seedmask=np.zeros((128, 6), np.float32))
        in_maps.append(m)
    return in_maps


L1_EXT = ["d_x1", "d_gate", "d_hbcpre", "d_dt", "d_wbT"]
L2_IN = ["d_hbcpre", "d_dt"]
L2_EXT = ["d_hs", "d_BC", "d_ls", "d_prep", "d_acsf", "d_dbc", "d_stot", "d_aseg"]
L3_IN = ["d_x1", "d_gate", "d_hs", "d_BC", "d_ls", "d_prep", "d_acsf", "d_dbc",
         "d_wbT", "d_ps0", "d_ps1", "d_ps2", "d_pa0", "d_pa1", "d_pa2"]


def kernel(**inputs):
    cfg1 = dict(stages=["ln0", "f1", "ln1", "i1nc", "dtdump"], extern=L1_EXT)
    cfg2 = dict(stages=["dtload", "convonly", "ssda", "stotdump"],
                extin=L2_IN, extern=L2_EXT)
    cfg3 = dict(stages=["seedcombine", "ssdmain_seeded", "gating", "outproj",
                        "ln2", "f2"], extin=L3_IN)
    base = make_in_maps(inputs, cfg1)

    nc1, _ = build_nc(cfg1)
    r1 = run_bass_kernel_spmd(nc1, base, core_ids=list(range(8)))

    maps2 = []
    for core in range(8):
        b, seg = divmod(core, 4)
        m = dict(base[core])
        m["d_hbcpre"] = r1.results[core]["d_hbcpre"]
        m["d_dt"] = r1.results[core]["d_dt"]
        if seg == 0:
            m["halo_bf"] = np.zeros((CONVD, 3), BFNP)
        else:
            m["halo_bf"] = np.ascontiguousarray(
                r1.results[core - 1]["d_hbcpre"][:, T - 3:T])
        maps2.append(m)
    nc2, _ = build_nc(cfg2)
    r2 = run_bass_kernel_spmd(nc2, maps2, core_ids=list(range(8)))

    maps3 = []
    for core in range(8):
        b, seg = divmod(core, 4)
        m = dict(base[core])
        for k in ("d_x1", "d_gate", "d_wbT"):
            m[k] = r1.results[core][k]
        for k in ("d_hs", "d_BC", "d_ls", "d_prep", "d_acsf", "d_dbc"):
            m[k] = r2.results[core][k]
        sm = np.zeros((128, 6), np.float32)
        for j in range(3):
            m[f"d_ps{j}"] = np.zeros((NS, INTER), np.float32)
            m[f"d_pa{j}"] = np.zeros((128, 1), np.float32)
        for idx, src_seg in enumerate(range(seg)):
            j = 3 - seg + idx
            src = 4 * b + src_seg
            m[f"d_ps{j}"] = r2.results[src]["d_stot"]
            m[f"d_pa{j}"] = r2.results[src]["d_aseg"]
            sm[:, j] = 1.0
            sm[:, 3 + j] = 1.0
        m["seedmask"] = sm
        maps3.append(m)
    nc3, _ = build_nc(cfg3)
    r3 = run_bass_kernel_spmd(nc3, maps3, core_ids=list(range(8)))

    out = np.empty((2, 8192, D), np.float32)
    for core in range(8):
        b, seg = divmod(core, 4)
        out[b, seg * T:(seg + 1) * T, :] = r3.results[core]["outT"].T
    return out
